# version 01
# baseline (speedup 1.0000x reference)
import numpy as np
import concourse.bass as bass
import concourse.mybir as mybir
from concourse.bass_utils import run_bass_kernel_spmd

F32 = mybir.dt.float32
BF16 = mybir.dt.bfloat16
AF = mybir.ActivationFunctionType
ALU = mybir.AluOpType
AX = mybir.AxisListType

S = 4096
D = 1024
T = 32
NSUP = 8
E = 8
H = 2816
HCN = 22
DK = 8
KD = 64
NH = 16
EPS = 1e-5

N_SUB = 4
ONLY = None
NSUP_LIM = NSUP
DBG = []
LAST_RESULTS = None

FFN_BLOCKS = [(i * 3, 3) for i in range(10)] + [(30, 2)]


def _ffn_blocks():
    return FFN_BLOCKS

SB = {}
_off = [16512]


def _alloc(name, bytes_per_part):
    o = _off[0]
    SB[name] = o
    _off[0] = o + ((bytes_per_part + 31) // 32) * 32
    return o


for _n, _b in [("ones_b", 256), ("ones_f", 512), ("ones64_b", 32),
               ("idb", 256), ("idf", 512), ("maskT", 256),
               ("m_sb", 128), ("rstd_sb", 128), ("var_sb", 128),
               ("ssum", 128), ("ssq", 128), ("wcol", 1024),
               ("s8", 32), ("s8b", 32), ("s8c", 32), ("t128a", 32),
               ("t128b", 32), ("wq_c", 1024), ("wk_c", 1024), ("wv1_c", 1024),
               ("wv2_c", 2048), ("wg_c", 128), ("bias_a", 32),
               ("sc0", 2048), ("sc1", 2048), ("tiny", 256), ("epsb", 32),
               ("wgf", 256)]:
    _alloc(_n, _b)
C_END = _off[0]
assert C_END <= 30080, C_END

OFF_A = 30080
SB["xs"] = OFF_A
OFF_B = OFF_A + 131072
B_END = 229344
assert OFF_B + 65536 <= B_END

SB["hb32"] = OFF_B
SB["hTb"] = OFF_B + 4096
SB["guT"] = SB["hTb"] + 6144
SB["sg0"] = SB["guT"] + 16896
SB["sg1"] = SB["sg0"] + 1536
SB["w12b0"] = SB["sg1"] + 1536
SB["w12b1"] = SB["w12b0"] + 4096
SB["w3b0"] = SB["w12b1"] + 4096
SB["w3b1"] = SB["w3b0"] + 2048
SB["hT32"] = SB["w3b1"] + 2048
assert SB["hT32"] + 4096 <= B_END
SB["hTs"] = OFF_B + 4096
SB["qT"] = SB["hTs"] + 8192
SB["kT"] = SB["qT"] + 8192
SB["vTs"] = SB["kT"] + 8192
SB["v64"] = SB["vTs"] + 2048
SB["pT0"] = SB["v64"] + 4160
SB["pT1"] = SB["pT0"] + 1024
SB["osb"] = SB["pT1"] + 1024
SB["oT"] = SB["osb"] + 2048
SB["q2"] = SB["hTs"]
assert SB["oT"] + 1024 <= B_END
SB["hT"] = OFF_B
SB["hbb0"] = OFF_B + 65536
SB["hbb1"] = OFF_B + 66048
assert SB["hbb1"] + 512 <= B_END
A0 = OFF_A
SB["kTg"] = A0
SB["vg"] = A0 + 32768
SB["qTs"] = SB["vg"] + 33280
SB["oTs"] = SB["qTs"] + 4096
SB["Qw"] = SB["oTs"] + 8192
SB["Wrhs"] = SB["Qw"] + 8192
SB["kvw0"] = SB["Wrhs"] + 16384
SB["kvw1"] = SB["kvw0"] + 2048
SB["zb0"] = SB["kvw1"] + 2048
SB["zb1"] = SB["zb0"] + 4096
SB["pTq"] = SB["zb1"] + 4096
SB["osb_b"] = SB["pTq"] + 4096
SB["vTt"] = SB["osb_b"] + 2048
assert SB["vTt"] + 4096 <= A0 + 131072


class Sem:
    def __init__(self, handle, name):
        self.h = handle
        self.name = name
        self.n = 0


class Ev:
    __slots__ = ("eng", "fn", "deps", "inc", "val", "sem")

    def __init__(self, eng, fn, deps=(), inc=0, sem=None):
        self.eng = eng
        self.fn = fn
        self.deps = [d for d in deps if d is not None]
        self.inc = inc
        self.sem = sem
        self.val = None


class Prog:
    def __init__(self, nc, sems):
        self.nc = nc
        self.sems = sems
        self.events = []

    def ev(self, eng, fn, deps=(), inc=0, sem=None):
        e = Ev(eng, fn, deps, inc, sem)
        self.events.append(e)
        return e

    def t(self, fn, deps=(), inc=1):
        return self.ev("tensor", fn, deps, inc, self.sems["TM"])

    def v(self, fn, deps=(), inc=1):
        return self.ev("vector", fn, deps, inc, self.sems["VM"])

    def s(self, fn, deps=(), inc=1):
        return self.ev("scalar", fn, deps, inc, self.sems["SM"])

    def dma(self, eng, fn, deps, sem):
        return self.ev(eng, fn, deps, inc=16, sem=sem)

    def flush(self):
        events = self.events
        self.events = []
        for e in events:
            if e.inc:
                e.sem.n += e.inc
                e.val = e.sem.n
        with self.nc.Block() as block:
            for eng_name in ["sync", "scalar", "vector", "tensor", "gpsimd"]:
                evs = [e for e in events if e.eng == eng_name]
                if not evs:
                    continue

                def make(eng_name, evs):
                    def body(eng):
                        waited = {}
                        for e in evs:
                            for d in e.deps:
                                assert d.inc, (
                                    "dep on non-milestone event",
                                    e.eng, d.eng, d.fn.__code__.co_filename,
                                    d.fn.__code__.co_firstlineno,
                                    e.fn.__code__.co_firstlineno)
                                if waited.get(id(d.sem), -1) < d.val:
                                    eng.wait_ge(d.sem.h, d.val)
                                    waited[id(d.sem)] = d.val
                            inst = e.fn(eng)
                            if e.inc:
                                inst.then_inc(e.sem.h, e.inc)
                    return body

                getattr(block, eng_name)(make(eng_name, evs))


def build_program():
    nc = bass.Bass("TRN2", target_bir_lowering=False)

    x_in = nc.dram_tensor("x_in", [S, D], F32, kind="ExternalInput").ap()
    out = nc.dram_tensor("out", [S, D], F32, kind="ExternalOutput").ap()
    wq = nc.dram_tensor("wq", [D, KD], F32, kind="ExternalInput").ap()
    wk = nc.dram_tensor("wk", [D, KD], F32, kind="ExternalInput").ap()
    wv1 = nc.dram_tensor("wv1", [D, KD], F32, kind="ExternalInput").ap()
    wv2 = nc.dram_tensor("wv2", [KD, D], F32, kind="ExternalInput").ap()
    qh = nc.dram_tensor("qh", [8, DK, 128, 128], F32, kind="ExternalInput").ap()
    kh = nc.dram_tensor("kh", [8, DK, 128, 128], F32, kind="ExternalInput").ap()
    vh = nc.dram_tensor("vh", [8, DK, 128, 128], F32, kind="ExternalInput").ap()
    wo = nc.dram_tensor("wo", [NH, KD, D], F32, kind="ExternalInput").ap()
    wgt = nc.dram_tensor("wgt", [D, E], F32, kind="ExternalInput").ap()
    w1h = nc.dram_tensor("w1h", [E, HCN, DK, 128, 128], F32, kind="ExternalInput").ap()
    w2h = nc.dram_tensor("w2h", [E, HCN, DK, 128, 128], F32, kind="ExternalInput").ap()
    w3h = nc.dram_tensor("w3h", [E, HCN, 128, D], F32, kind="ExternalInput").ap()
    ident = nc.dram_tensor("ident", [128, 128], F32, kind="ExternalInput").ap()
    mask = nc.dram_tensor("mask", [128, 128], F32, kind="ExternalInput").ap()
    onesv = nc.dram_tensor("onesv", [128, 128], F32, kind="ExternalInput").ap()

    x_sp = nc.dram_tensor("x_sp", [S, D], F32, kind="Internal").ap()
    w12s = nc.dram_tensor("w12s", [E * HCN, 128, 2048], BF16, kind="Internal").ap()
    w3s = nc.dram_tensor("w3s", [E * HCN, 128, 1024], BF16, kind="Internal").ap()

    TENS = {}

    def sbt(name, shape, dtype):
        ap = nc.alloc_sbuf_tensor_at(name, shape, dtype, offset=SB[name]).ap()
        TENS[name] = ap
        return ap

    ones_b = sbt("ones_b", (128, 128), BF16)
    ones_f = sbt("ones_f", (128, 128), F32)
    ones64_b = sbt("ones64_b", (64, 1), BF16)
    idb = sbt("idb", (128, 128), BF16)
    idf = sbt("idf", (128, 128), F32)
    maskT = sbt("maskT", (128, 128), BF16)
    m_sb = sbt("m_sb", (128, 32), F32)
    rstd_sb = sbt("rstd_sb", (128, 32), F32)
    var_sb = sbt("var_sb", (128, 32), F32)
    ssum = sbt("ssum", (128, 32), F32)
    ssq = sbt("ssq", (128, 32), F32)
    wcol = sbt("wcol", (128, 256), F32)
    s8 = sbt("s8", (128, 8), F32)
    s8b = sbt("s8b", (128, 8), F32)
    s8c = sbt("s8c", (128, 8), F32)
    t128a = sbt("t128a", (128, 1), F32)
    t128b = sbt("t128b", (128, 1), F32)
    wq_c = sbt("wq_c", (128, 512), BF16)
    wk_c = sbt("wk_c", (128, 512), BF16)
    wv1_c = sbt("wv1_c", (128, 512), BF16)
    wv2_c = sbt("wv2_c", (64, 1024), BF16)
    wg_c = sbt("wg_c", (128, 64), BF16)
    bias_a = sbt("bias_a", (128, 1), F32)
    sc0 = sbt("sc0", (128, 512), F32)
    sc1 = sbt("sc1", (128, 512), F32)
    tiny = sbt("tiny", (1, 64), F32)
    epsb = sbt("epsb", (128, 1), F32)
    xs = sbt("xs", (128, T * D), F32)
    hb32 = sbt("hb32", (128, 1024), F32)
    hTb = sbt("hTb", (128, DK * 384), BF16)
    guT = sbt("guT", (128, HCN * 384), BF16)
    sg = [sbt("sg0", (128, 384), F32), sbt("sg1", (128, 384), F32)]
    w12b = [sbt("w12b0", (128, 2048), BF16), sbt("w12b1", (128, 2048), BF16)]
    w3b = [sbt("w3b0", (128, 1024), BF16), sbt("w3b1", (128, 1024), BF16)]
    hTs = sbt("hTs", (128, DK * 512), BF16)
    qT = sbt("qT", (64, S), BF16)
    kT = sbt("kT", (64, S), BF16)
    vTs = sbt("vTs", (64, 512), F32)
    v64 = sbt("v64", (128, T * 65), BF16)
    pT = [sbt("pT0", (128, 512), BF16), sbt("pT1", (128, 512), BF16)]
    osb = sbt("osb", (64, 512), F32)
    oT = sbt("oT", (64, 512), BF16)
    q2 = sbt("q2", (64, S), BF16)
    hT = sbt("hT", (128, DK * S), BF16)
    hbb = [sbt("hbb0", (128, 128), F32), sbt("hbb1", (128, 128), F32)]
    kTg = sbt("kTg", (128, 4 * S), BF16)
    vg = sbt("vg", (128, T * 8 * 65), BF16)
    qTs = sbt("qTs", (128, 4 * 512), BF16)
    oTs = sbt("oTs", (64, 8 * 512), BF16)
    Qw = sbt("Qw", (128, 4 * 1024), BF16)
    Wrhs = sbt("Wrhs", (64, 8 * 1024), BF16)
    kvw = [sbt("kvw0", (128, 1024), BF16), sbt("kvw1", (128, 1024), BF16)]
    zb = [sbt("zb0", (128, 1024), F32), sbt("zb1", (128, 1024), F32)]
    pTq = sbt("pTq", (128, 2048), BF16)
    osb_b = sbt("osb_b", (64, 512), F32)
    vTt = sbt("vTt", (128, 1024), F32)
    hT32 = sbt("hT32", (128, 1024), F32)
    wgf = sbt("wgf", (128, 64), F32)

    def xs_t(t):
        return xs[:, D * t:D * (t + 1)]

    sem_names = ["TM", "VM", "SM",
                 "d_c", "d_x", "d_w12a", "d_w12b", "d_w3a", "d_w3b", "d_wb",
                 "d_c12a", "d_c12b", "d_c3a", "d_c3b",
                 "d_sp", "d_kva", "d_kvb", "d_q", "d_rmwa", "d_rmwb",
                 "d_rel", "d_out"]
    sem_ctxs = {n: nc.semaphore(n) for n in sem_names}
    sems = {n: Sem(c.__enter__(), n) for n, c in sem_ctxs.items()}
    P = Prog(nc, sems)

    dc = sems["d_c"]

    def gd(dst, src):
        return lambda g: g.dma_start(out=dst, in_=src)

    consts = [
        P.dma("gpsimd", gd(idb[:, :], ident[:, :]), (), dc),
        P.dma("gpsimd", gd(idf[:, :], ident[:, :]), (), dc),
        P.dma("gpsimd", gd(maskT[:, :], mask[:, :]), (), dc),
        P.dma("gpsimd", gd(ones_b[:, :], onesv[:, :]), (), dc),
        P.dma("gpsimd", gd(ones_f[:, :], onesv[:, :]), (), dc),
        P.dma("gpsimd", lambda g: g.dma_start(out=ones64_b[:, :],
            in_=onesv[0:64, 0:1], allow_slow_non_contiguous=True), (), dc),
    ]
    for (dst, src) in ((wq_c, wq), (wk_c, wk), (wv1_c, wv1)):
        consts.append(P.dma(
            "gpsimd",
            gd(dst.rearrange("p (k c) -> p k c", k=DK),
               src.rearrange("(k p) c -> p k c", p=128)), (), dc))
    consts.append(P.dma("gpsimd", gd(wv2_c[:, :], wv2[:, :]), (), dc))
    consts.append(P.dma(
        "gpsimd",
        gd(wg_c.rearrange("p (k e) -> p k e", k=DK),
           wgt.rearrange("(k p) e -> p k e", p=128)), (), dc))
    consts.append(P.dma(
        "gpsimd",
        gd(wgf.rearrange("p (k e) -> p k e", k=DK),
           wgt.rearrange("(k p) e -> p k e", p=128)), (), dc))
    dx = sems["d_x"]
    xload = []
    for i in range(8):
        xload.append(P.dma(
            "sync",
            lambda s, i=i: s.dma_start(
                out=xs.rearrange("p (t d) -> p t d", t=T)[:, 4 * i:4 * i + 4, :],
                in_=x_in.rearrange("(t p) d -> p t d", p=128)[:, 4 * i:4 * i + 4, :]),
            (), dx))
    e_consts = consts[-1]
    e_xload = xload[-1]
    e_eps = P.v(lambda v: v.memset(epsb[:, :], EPS), deps=(), inc=1)
    for nm in DBG:
        P.v(lambda v, nm=nm: v.memset(TENS[nm][:, :], 0.0), deps=(), inc=1)

    def stats(first_deps):
        eq = er = None
        for t in range(T):
            eh = P.v(lambda v, t=t: v.tensor_tensor(hb32[:, :], xs_t(t), xs_t(t),
                                                    ALU.mult),
                     deps=first_deps if t == 0 else (), inc=1)
            eq = P.v(lambda v, t=t: v.reduce_sum(ssq[:, t:t + 1], hb32[:, :],
                                                 axis=AX.X), deps=[eh], inc=1)
            er = P.v(lambda v, t=t: v.reduce_sum(ssum[:, t:t + 1], xs_t(t),
                                                 axis=AX.X), inc=1)
        e1 = P.v(lambda v: v.tensor_scalar_mul(m_sb[:, :], ssum[:, :], 1.0 / D),
                 deps=[er], inc=1)
        e2 = P.v(lambda v: v.tensor_scalar_mul(ssq[:, :], ssq[:, :], 1.0 / D),
                 deps=[eq], inc=1)
        e3 = P.v(lambda v: v.tensor_tensor(var_sb[:, :], m_sb[:, :], m_sb[:, :],
                                           ALU.mult), deps=[e1], inc=1)
        ev = P.v(lambda v: v.tensor_sub(var_sb[:, :], ssq[:, :], var_sb[:, :]),
                 deps=[e2, e3], inc=1)
        el = P.s(lambda s: s.activation(var_sb[:, :], var_sb[:, :], AF.Ln,
                                        bias=epsb[:, :]), deps=[ev, e_eps], inc=1)
        return P.s(lambda s: s.activation(rstd_sb[:, :], var_sb[:, :], AF.Exp,
                                          scale=-0.5), deps=[el], inc=1)

    def ln_transpose(t, e_rstd, p_tr, dst_view, tcol, prev_tr, prev_ev,
                     extra_hb_deps=()):
        deps_h = list(extra_hb_deps)
        if prev_tr is not None:
            deps_h.append(prev_tr)
        if e_rstd is not None:
            deps_h.append(e_rstd)
        e_hb = P.v(lambda v, t=t: v.tensor_scalar(
            hb32[:, :], xs_t(t), m_sb[:, t:t + 1], rstd_sb[:, t:t + 1],
            ALU.subtract, ALU.mult), deps=deps_h, inc=1)
        evs = [None, None]
        e_tr = None
        for r in range(2):
            for kk in range(4):
                k = 4 * r + kk
                dps = []
                if kk == 0:
                    dps = [e_hb]
                    if prev_ev[r] is not None:
                        dps.append(prev_ev[r])
                e_tr = P.t(lambda t_, k=k, kk=kk, r=r: t_.transpose(
                    p_tr[r][:, 128 * kk:128 * (kk + 1)],
                    hb32[:, 128 * k:128 * (k + 1)], idf[:, :]),
                    deps=dps, inc=(kk == 3))
            evs[r] = P.s(lambda s, r=r, tcol=tcol: s.copy(
                dst_view[:, 4 * r:4 * r + 4, tcol:tcol + 128],
                p_tr[r].rearrange("p (k c) -> p k c", k=4)),
                deps=[e_tr], inc=1)
        return e_hb, e_tr, evs

    def emit_attn_a():
        e_rstd = stats([e_xload])
        hTs_v = hTs.rearrange("p (k c) -> p k c", k=DK)
        v64_v = v64.rearrange("p (t c) -> p t c", t=T)
        with (
            nc.psum_tensor("pA_tr0", [128, 512], F32) as p_tr0,
            nc.psum_tensor("pA_tr1", [128, 512], F32) as p_tr1,
            nc.psum_tensor("pA_q", [128, 512], F32) as p_q_,
            nc.psum_tensor("pA_k", [128, 512], F32) as p_k_,
            nc.psum_tensor("pA_v", [128, 512], F32) as p_v_,
            nc.psum_tensor("pA_n", [128, 512], F32) as p_n_,
        ):
            p_tr = [p_tr0.ap(), p_tr1.ap()]
            p_q, p_k, p_v, p_n = p_q_.ap(), p_k_.ap(), p_v_.ap(), p_n_.ap()
            prev_tr = None
            prev_ev = [None, None]
            e_q_ev = e_k_ev = e_vTs_ev = e_vg_ev = None
            for I in range(NSUP):
                for tt in range(4):
                    t = 4 * I + tt
                    e_hb, e_tr, evs = ln_transpose(
                        t, e_rstd if t == 0 else None, p_tr, hTs_v, 128 * tt,
                        prev_tr, prev_ev,
                        extra_hb_deps=[e_xload] if t == 0 else ())
                    prev_tr, prev_ev = e_tr, evs
                e_hT = evs[1]
                dep0 = [e_hT, evs[0]] + ([e_q_ev] if e_q_ev else []) + \
                       ([e_consts] if I == 0 else [])
                for k in range(DK):
                    e_qmm = P.t(lambda t_, k=k: t_.matmul(
                        p_q[0:64, :], wq_c[:, 64 * k:64 * (k + 1)], hTs_v[:, k, :],
                        start=(k == 0), stop=(k == DK - 1)),
                        deps=dep0 if k == 0 else (), inc=(k == DK - 1))
                e_q_ev = P.s(lambda s, I=I: s.copy(qT[:, 512 * I:512 * (I + 1)],
                                                   p_q[0:64, :]), deps=[e_qmm], inc=1)
                dep0 = [e_k_ev] if e_k_ev else []
                for k in range(DK):
                    e_kmm = P.t(lambda t_, k=k: t_.matmul(
                        p_k[0:64, :], wk_c[:, 64 * k:64 * (k + 1)], hTs_v[:, k, :],
                        start=(k == 0), stop=(k == DK - 1)),
                        deps=dep0 if k == 0 else (), inc=(k == DK - 1))
                e_k_ev = P.s(lambda s, I=I: s.copy(kT[:, 512 * I:512 * (I + 1)],
                                                   p_k[0:64, :]), deps=[e_kmm], inc=1)
                dep0 = [e_vTs_ev] if e_vTs_ev else []
                for k in range(DK):
                    e_vmm = P.t(lambda t_, k=k: t_.matmul(
                        p_v[0:64, :], wv1_c[:, 64 * k:64 * (k + 1)], hTs_v[:, k, :],
                        start=(k == 0), stop=(k == DK - 1)),
                        deps=dep0 if k == 0 else (), inc=(k == DK - 1))
                e_vTs_ev = P.s(lambda s: s.copy(vTs[:, :], p_v[0:64, :]),
                               deps=[e_vmm], inc=1)
                for tt in range(4):
                    dps = [e_vTs_ev] + ([e_vg_ev] if e_vg_ev else [])
                    e_vtr = P.t(lambda t_, tt=tt: t_.transpose(
                        p_n[:, 64 * tt:64 * (tt + 1)],
                        vTs[0:64, 128 * tt:128 * (tt + 1)], idf[0:64, 0:64]),
                        deps=dps, inc=1)
                    e_vg_ev = P.s(lambda s, I=I, tt=tt: s.copy(
                        v64_v[:, 4 * I + tt, 0:64], p_n[:, 64 * tt:64 * (tt + 1)]),
                        deps=[e_vtr], inc=1)
            e_ones = P.v(lambda v: v.memset(v64_v[:, :, 64:65], 1.0),
                         deps=[e_vg_ev], inc=1)

            e_sq = P.v(lambda v: v.tensor_tensor(q2[:, :], qT[:, :], qT[:, :],
                                                 ALU.mult),
                       deps=[e_q_ev, e_k_ev, e_vg_ev], inc=1)
            e_red = None
            for c in range(8):
                e_nmm = P.t(lambda t_, c=c: t_.matmul(
                    p_n[0:1, :], ones64_b[:, :], q2[0:64, 512 * c:512 * (c + 1)],
                    start=True, stop=True),
                    deps=[e_sq] + ([e_red] if e_red else []) +
                         ([e_vg_ev] if c == 0 else []), inc=1)
                e_red = P.v(lambda v, c=c: v.reduce_max(
                    tiny[:, c:c + 1], p_n[0:1, :], axis=AX.X), deps=[e_nmm], inc=1)
            e_sk = P.v(lambda v: v.tensor_tensor(q2[:, :], kT[:, :], kT[:, :],
                                                 ALU.mult), deps=[e_k_ev, e_red], inc=1)
            for c in range(8):
                e_nmm = P.t(lambda t_, c=c: t_.matmul(
                    p_n[0:1, :], ones64_b[:, :], q2[0:64, 512 * c:512 * (c + 1)],
                    start=True, stop=True), deps=[e_sk, e_red], inc=1)
                e_red = P.v(lambda v, c=c: v.reduce_max(
                    tiny[:, 8 + c:9 + c], p_n[0:1, :], axis=AX.X),
                    deps=[e_nmm], inc=1)
            ea = P.v(lambda v: v.reduce_max(tiny[:, 16:17], tiny[:, 0:8],
                                            axis=AX.X), deps=[e_red], inc=1)
            eb = P.v(lambda v: v.reduce_max(tiny[:, 17:18], tiny[:, 8:16],
                                            axis=AX.X), deps=[e_red], inc=1)
            e_pr = P.v(lambda v: v.tensor_tensor(tiny[:, 18:19], tiny[:, 16:17],
                                                 tiny[:, 17:18], ALU.mult),
                       deps=[ea, eb], inc=1)
            eln = P.s(lambda s: s.activation(tiny[:, 19:20], tiny[:, 18:19], AF.Ln),
                      deps=[e_pr], inc=1)
            e_c = P.s(lambda s: s.activation(tiny[:, 20:21], tiny[:, 19:20], AF.Exp,
                                             scale=0.5), deps=[eln], inc=1)
            e_nb = P.v(lambda v: v.tensor_scalar_mul(tiny[:, 21:22], tiny[:, 20:21],
                                                     -0.125), deps=[e_c], inc=1)
            e_rep = P.t(lambda t_: t_.matmul(p_n[:, 0:1], ones_f[0:1, :],
                                             tiny[0:1, 21:22], start=True, stop=True),
                        deps=[e_nb, e_red], inc=1)
            e_bias = P.s(lambda s: s.copy(bias_a[:, :], p_n[:, 0:1]),
                         deps=[e_rep], inc=1)

        evicts = []
        with (
            nc.psum_tensor("pA_s0", [128, 512], F32) as p_s0,
            nc.psum_tensor("pA_s1", [128, 512], F32) as p_s1,
            nc.psum_tensor("pA_o", [128, 512], F32) as p_o_,
            nc.psum_tensor("pA_l", [128, 512], F32) as p_l_,
            nc.psum_tensor("pA_z", [128, 1024], F32) as p_z_,
        ):
            p_s = [p_s0.ap(), p_s1.ap()]
            p_o, p_l, p_z = p_o_.ap(), p_l_.ap(), p_z_.ap()
            e_exp = {}
            e_om = {}
            e_linv = e_ocp = e_lrep = e_ot = e_zev = None
            for I in range(NSUP):
                nj = 4 * I + 4
                for j in range(nj):
                    par = j % 2
                    deps_s = []
                    if I == 0 and j == 0:
                        deps_s += [e_bias, e_ones]
                    if j >= 2:
                        deps_s.append(e_exp[(I, j - 2)])
                    elif I > 0:
                        deps_s.append(e_exp[(I - 1, 4 * I - 2 + par)])
                    if j == 0 and e_ocp is not None:
                        deps_s += [e_ocp, e_linv]
                    e_s = P.t(lambda t_, j=j, I=I, par=par: t_.matmul(
                        p_s[par][:, :], kT[:, 128 * j:128 * (j + 1)],
                        qT[:, 512 * I:512 * (I + 1)], start=True, stop=True),
                        deps=deps_s, inc=1)
                    deps_e = [e_s]
                    if j >= 2:
                        deps_e.append(e_om[(I, j - 2)])
                    elif I > 0:
                        deps_e.append(e_om[(I - 1, 4 * I - 2 + par)])
                    e_x = P.s(lambda s, par=par: s.activation(
                        pT[par][:, :], p_s[par][:, :], AF.Exp, bias=bias_a[:, :],
                        scale=0.125), deps=deps_e, inc=1)
                    e_exp[(I, j)] = e_x
                    m = j - 4 * I
                    e_pr2 = e_x
                    if m >= 0:
                        if m > 0:
                            P.v(lambda v, m=m, par=par: v.memset(
                                pT[par][:, 0:128 * m], 0.0), deps=[e_x], inc=0)
                        e_pr2 = P.v(lambda v, m=m, par=par: v.tensor_tensor(
                            pT[par][:, 128 * m:128 * (m + 1)],
                            pT[par][:, 128 * m:128 * (m + 1)], maskT[:, :],
                            ALU.mult), deps=[e_x], inc=1)
                    e_o = P.t(lambda t_, j=j, par=par, nj=nj: t_.matmul(
                        p_o[0:65, :], v64_v[:, j, :], pT[par][:, :],
                        start=(j == 0), stop=(j == nj - 1)), deps=[e_pr2], inc=1)
                    e_om[(I, j)] = e_o
                e_linv = P.v(lambda v: v.reciprocal(sc0[64:65, :], p_o[64:65, :]),
                             deps=[e_o] + ([e_lrep] if e_lrep else []), inc=1)
                e_lrep = P.t(lambda t_: t_.matmul(
                    p_l[:, :], ones_f[64:65, :], sc0[64:65, :],
                    start=True, stop=True), deps=[e_linv], inc=1)
                e_ocp = P.s(lambda s: s.copy(osb[:, :], p_o[0:64, :]),
                            deps=[e_o] + ([e_ot] if e_ot else []), inc=1)
                e_ot = P.v(lambda v: v.tensor_tensor(oT[:, :], osb[:, :],
                                                     p_l[0:64, :], ALU.mult),
                           deps=[e_lrep, e_ocp], inc=1)
                for tt in range(4):
                    t = 4 * I + tt
                    for n in range(2):
                        e_z = P.t(lambda t_, tt=tt, n=n: t_.matmul(
                            p_z[:, 512 * n:512 * (n + 1)],
                            oT[:, 128 * tt:128 * (tt + 1)],
                            wv2_c[:, 512 * n:512 * (n + 1)], start=True, stop=True),
                            deps=([e_ot] + ([e_zev] if e_zev else []))
                            if n == 0 else (), inc=(n == 1))
                    e_zev = P.v(lambda v, t=t: v.tensor_tensor(
                        xs_t(t), xs_t(t), p_z[:, :], ALU.add), deps=[e_z], inc=1)
                    evicts.append(e_zev)
        return evicts

    ffn_state = {"wb_last": None}

    def emit_ffn(call_idx, stats_deps):
        e_rstd = stats(stats_deps)
        hTb_v = hTb.rearrange("p (k c) -> p k c", k=DK)
        d_w12 = [sems["d_w12a"], sems["d_w12b"]]
        d_w3 = [sems["d_w3a"], sems["d_w3b"]]
        d_c12 = [sems["d_c12a"], sems["d_c12b"]]
        d_c3 = [sems["d_c3a"], sems["d_c3b"]]
        dwb = sems["d_wb"]
        use_cast = (call_idx == 1)
        evicts_all = []
        with (
            nc.psum_tensor(f"pF_g{call_idx}", [128, 512], F32) as p_g_,
            nc.psum_tensor(f"pF_u{call_idx}", [128, 512], F32) as p_u_,
            nc.psum_tensor(f"pF_o{call_idx}", [128, 3072], F32) as p_o_,
        ):
            p_g, p_u, p_o = p_g_.ap(), p_u_.ap(), p_o_.ap()
            e_gu = {}
            e_silu = {}
            gci = -1
            e_last_tr = None
            e_last_evp = None
            e_w12_reader = [None, None]
            e_w3_reader = [None, None]
            e_evict_last = None
            scratch_dep_pending = [True]
            for bi, (t0, ntiles) in enumerate(_ffn_blocks()):
                tok = 128 * ntiles
                cast_blk = use_cast and bi == 0
                e_hT = None
                e_s8cp = None
                e_top_done = None
                for tt in range(ntiles):
                    t = t0 + tt
                    deps_h = ([e_rstd] if (bi == 0 and tt == 0) else [])
                    if e_last_tr is not None:
                        deps_h.append(e_last_tr)
                    e_hb = P.v(lambda v, t=t: v.tensor_scalar(
                        hb32[:, :], xs_t(t), m_sb[:, t:t + 1], rstd_sb[:, t:t + 1],
                        ALU.subtract, ALU.mult), deps=deps_h, inc=1)
                    for r in range(2):
                        for kk in range(4):
                            k = 4 * r + kk
                            dps = []
                            if kk == 0:
                                dps = [e_hb]
                                if r == 0 and tt == 0 and gci >= 0:
                                    dps.append(e_gu[gci])
                                if e_last_evp is not None:
                                    dps.append(e_last_evp)
                            e_tr = P.t(lambda t_, k=k, kk=kk: t_.transpose(
                                p_u[:, 128 * kk:128 * (kk + 1)],
                                hb32[:, 128 * k:128 * (k + 1)], idf[:, :]),
                                deps=dps, inc=(kk == 3))
                        P.s(lambda s, r=r, tt=tt: s.copy(
                            hTb_v[:, 4 * r:4 * r + 4, 128 * tt:128 * (tt + 1)],
                            p_u.rearrange("p (k c) -> p k c", k=4)),
                            deps=[e_tr], inc=0)
                        e_last_evp = P.s(lambda s, r=r: s.copy(
                            hT32[:, 512 * r:512 * (r + 1)], p_u[:, :]),
                            deps=[e_tr], inc=1)
                    e_last_tr = e_tr
                    e_hT = e_last_evp
                    sc_deps = [e_hT]
                    if tt == 0 and gci >= 0:
                        sc_deps.append(e_silu[gci])
                    if tt > 0:
                        sc_deps.append(e_s8cp)
                    if bi == 0 and tt == 0 and call_idx == 1:
                        sc_deps.append(e_consts)
                    for k in range(DK):
                        e_smm = P.t(lambda t_, k=k: t_.matmul(
                            p_g[:, 0:8], hT32[:, 128 * k:128 * (k + 1)],
                            wgf[:, 8 * k:8 * (k + 1)], start=(k == 0),
                            stop=(k == DK - 1)),
                            deps=sc_deps if k == 0 else (), inc=(k == DK - 1))
                    e_s8cp = P.v(lambda v: v.tensor_copy(s8[:, :], p_g[:, 0:8]),
                                 deps=[e_smm], inc=1)
                    c1 = P.v(lambda v: v.reduce_max(t128a[:, :], s8[:, :],
                                                    axis=AX.X), deps=[e_s8cp], inc=1)
                    c2 = P.v(lambda v: v.tensor_scalar(s8b[:, :], s8[:, :],
                                                       t128a[:, :], -1e30,
                                                       ALU.is_equal, ALU.mult),
                             deps=[c1], inc=1)
                    c3 = P.v(lambda v: v.tensor_add(s8b[:, :], s8[:, :], s8b[:, :]),
                             deps=[c2], inc=1)
                    c4 = P.v(lambda v: v.reduce_max(t128b[:, :], s8b[:, :],
                                                    axis=AX.X), deps=[c3], inc=1)
                    c5 = P.v(lambda v: v.tensor_scalar(s8c[:, :], s8[:, :],
                                                       t128b[:, :], None,
                                                       ALU.is_ge), deps=[c4], inc=1)
                    e_top_done = P.v(lambda v, t=t: v.tensor_tensor(
                        wcol[:, 8 * t:8 * (t + 1)], s8[:, :], s8c[:, :], ALU.mult),
                        deps=[c5], inc=1)

                for e in range(E):
                    for hc in range(HCN):
                        ci = e * HCN + hc
                        gci += 1
                        par = ci % 2
                        if cast_blk:
                            dl = [e_w12_reader[par]] if e_w12_reader[par] else []
                            ev_a = P.dma("gpsimd", lambda g, e=e, hc=hc, par=par: g.dma_start(
                                out=w12b[par][:, 0:1024].rearrange("p (k c) -> p k c", k=DK),
                                in_=w1h[e, hc].rearrange("k p c -> p k c")),
                                deps=dl, sem=d_c12[par])
                            ev_b = P.dma("gpsimd", lambda g, e=e, hc=hc, par=par: g.dma_start(
                                out=w12b[par][:, 1024:2048].rearrange("p (k c) -> p k c", k=DK),
                                in_=w2h[e, hc].rearrange("k p c -> p k c")),
                                deps=(), sem=d_c12[par])
                            w12_ready = [ev_a, ev_b]
                        else:
                            dl = [e_w12_reader[par]] if e_w12_reader[par] else []
                            if scratch_dep_pending[0] and ffn_state["wb_last"] is not None:
                                dl.append(ffn_state["wb_last"])
                                scratch_dep_pending[0] = False
                            ev_a = P.dma("sync", lambda s, ci=ci, par=par: s.dma_start(
                                out=w12b[par][:, :], in_=w12s[ci]), deps=dl,
                                sem=d_w12[par])
                            w12_ready = [ev_a]
                        deps_g = list(w12_ready)
                        if gci >= 1:
                            deps_g.append(e_silu[gci - 1])
                        if hc == 0:
                            deps_g += [e_hT, e_top_done]
                            if e == 0 and e_evict_last is not None:
                                deps_g.append(e_evict_last)
                        for k in range(DK):
                            e_g = P.t(lambda t_, k=k, par=par, tok=tok: t_.matmul(
                                p_g[:, 0:tok], w12b[par][:, 128 * k:128 * (k + 1)],
                                hTb_v[:, k, 0:tok], start=(k == 0), stop=(k == DK - 1)),
                                deps=deps_g if k == 0 else (), inc=(k == DK - 1))
                        deps_u = []
                        if gci >= 1:
                            deps_u.append(e_gu[gci - 1])
                        for k in range(DK):
                            e_u = P.t(lambda t_, k=k, par=par, tok=tok: t_.matmul(
                                p_u[:, 0:tok],
                                w12b[par][:, 1024 + 128 * k:1024 + 128 * (k + 1)],
                                hTb_v[:, k, 0:tok], start=(k == 0), stop=(k == DK - 1)),
                                deps=deps_u if k == 0 else (), inc=(k == DK - 1))
                        e_w12_reader[par] = e_u
                        if cast_blk:
                            ffn_state["wb_last"] = P.dma(
                                "gpsimd", lambda g, ci=ci, par=par: g.dma_start(
                                    out=w12s[ci], in_=w12b[par][:, :]),
                                deps=[e_u], sem=dwb)
                        spar = gci % 2
                        deps_si = [e_g]
                        if gci >= 2:
                            deps_si.append(e_gu[gci - 2])
                        e_si = P.s(lambda s, spar=spar, tok=tok: s.activation(
                            sg[spar][:, 0:tok], p_g[:, 0:tok], AF.Silu),
                            deps=deps_si, inc=1)
                        e_silu[gci] = e_si
                        e_gu[gci] = P.v(lambda v, hc=hc, spar=spar, tok=tok: v.tensor_tensor(
                            guT[:, 384 * hc:384 * hc + tok], sg[spar][:, 0:tok],
                            p_u[:, 0:tok], ALU.mult), deps=[e_si, e_u], inc=1)
                    gci_e0 = gci - HCN + 1
                    for hc in range(HCN):
                        ci = e * HCN + hc
                        par = ci % 2
                        dl = [e_w3_reader[par]] if e_w3_reader[par] else []
                        if cast_blk:
                            ev3 = P.dma("gpsimd", lambda g, e=e, hc=hc, par=par: g.dma_start(
                                out=w3b[par][:, :], in_=w3h[e, hc]), deps=dl,
                                sem=d_c3[par])
                        else:
                            ev3 = P.dma("sync", lambda s, ci=ci, par=par: s.dma_start(
                                out=w3b[par][:, :], in_=w3s[ci]), deps=dl,
                                sem=d_w3[par])
                        deps_o = [ev3, e_gu[gci_e0 + hc]]
                        if hc == 0 and e > 0:
                            deps_o.append(e_evict_last)
                        for tt in range(ntiles):
                            for n in range(2):
                                e_omm = P.t(lambda t_, hc=hc, tt=tt, n=n, par=par,
                                            ntiles=ntiles: t_.matmul(
                                    p_o[:, 1024 * tt + 512 * n:1024 * tt + 512 * (n + 1)],
                                    guT[:, 384 * hc + 128 * tt:384 * hc + 128 * (tt + 1)],
                                    w3b[par][:, 512 * n:512 * (n + 1)],
                                    start=(hc == 0), stop=(hc == HCN - 1)),
                                    deps=deps_o if (tt == 0 and n == 0) else (),
                                    inc=(tt == ntiles - 1 and n == 1))
                        e_w3_reader[par] = e_omm
                        if cast_blk:
                            ffn_state["wb_last"] = P.dma(
                                "gpsimd", lambda g, ci=ci, par=par: g.dma_start(
                                    out=w3s[ci], in_=w3b[par][:, :]),
                                deps=[e_omm], sem=dwb)
                    for tt in range(ntiles):
                        t = t0 + tt
                        e_evict_last = P.v(lambda v, t=t, tt=tt, e=e: v.scalar_tensor_tensor(
                            xs_t(t), p_o[:, 1024 * tt:1024 * (tt + 1)],
                            wcol[:, 8 * t + e:8 * t + e + 1], xs_t(t),
                            ALU.mult, ALU.add),
                            deps=[e_omm] if tt == 0 else (), inc=1)
                        if e == E - 1:
                            evicts_all.append(e_evict_last)
        return evicts_all

    def emit_attn_b(prev_evict):
        e_rstd = stats([prev_evict])
        hT_v = hT.rearrange("p (k c) -> p k c", k=DK)
        vg_v = vg.rearrange("p (j h c) -> p j h c", j=T, h=8)
        dsp = sems["d_sp"]
        dkv = [sems["d_kva"], sems["d_kvb"]]
        dq = sems["d_q"]
        drmw = [sems["d_rmwa"], sems["d_rmwb"]]

        spills = []
        with (
            nc.psum_tensor("pB_tr0", [128, 512], F32) as p_tr0,
            nc.psum_tensor("pB_tr1", [128, 512], F32) as p_tr1,
        ):
            p_tr = [p_tr0.ap(), p_tr1.ap()]
            prev_tr = {0: None, 1: None}
            prev_pev = [None, None]
            e_hT_done = None
            for t in range(T):
                e_tr = None
                for r in range(2):
                    for kk in range(4):
                        k = 4 * r + kk
                        hpar = k % 2
                        dh = [e_rstd] if t == 0 and k == 0 else []
                        if prev_tr[hpar] is not None:
                            dh.append(prev_tr[hpar])
                        e_hk = P.v(lambda v, t=t, k=k, hpar=hpar: v.tensor_scalar(
                            hbb[hpar][:, :], xs_t(t)[:, 128 * k:128 * (k + 1)],
                            m_sb[:, t:t + 1], rstd_sb[:, t:t + 1],
                            ALU.subtract, ALU.mult), deps=dh, inc=1)
                        dt_ = [e_hk]
                        if kk == 0 and prev_pev[r] is not None:
                            dt_.append(prev_pev[r])
                        e_tr = P.t(lambda t_, kk=kk, r=r, hpar=hpar: t_.transpose(
                            p_tr[r][:, 128 * kk:128 * (kk + 1)],
                            hbb[hpar][:, :], idf[:, :]),
                            deps=dt_, inc=1)
                        prev_tr[hpar] = e_tr
                    prev_pev[r] = P.s(lambda s, r=r, t=t: s.copy(
                        hT_v[:, 4 * r:4 * r + 4, 128 * t:128 * (t + 1)],
                        p_tr[r].rearrange("p (k c) -> p k c", k=4)),
                        deps=[e_tr], inc=1)
                e_hT_done = prev_pev[1]
                e_hk_last = e_hk
                spills.append(P.dma("gpsimd", lambda g, t=t: g.dma_start(
                    out=x_sp.rearrange("(t p) d -> p t d", p=128)[:, t, :],
                    in_=xs_t(t)),
                    deps=[prev_evict] if t == 0 else (), sem=dsp))
        e_spill_done = spills[-1]

        rmw_evs = []
        rmw_by_par = {0: None, 1: None}
        e_qext = None
        e_oxp = None
        e_fa_tail = None
        e_qmm_last = None
        e_z_last = None
        e_zc = {0: None, 1: None}
        e_last_om = {0: None, 1: None}
        e_z_read_done = None
        e_exp_last = None
        with (
            nc.psum_tensor("pB_s", [128, 2048], F32) as p_s_,
            nc.psum_tensor("pB_o", [128, 2048], F32) as p_o_,
        ):
            p_sq, p_oq = p_s_.ap(), p_o_.ap()
            p_p = p_sq[:, 0:512]
            p_vt = p_oq[:, 0:512]
            p_z = p_oq[:, 0:1024]
            for grp in range(2):
                e_kv_reader = [None, None]
                e_pext = None
                e_vgev = None
                e_pp_reader = None
                for p in range(4):
                    pp = 4 * grp + p
                    par = p % 2
                    dl = [e_kv_reader[par]] if e_kv_reader[par] else []
                    if grp == 1 and p in (0, 1):
                        dl.append(e_fa_tail)
                    if grp == 0 and p == 0:
                        dl += [e_spill_done, e_hk_last]
                    ev_kw = P.dma("gpsimd", lambda g, pp=pp, par=par: g.dma_start(
                        out=kvw[par].rearrange("p (k c) -> p k c", k=DK),
                        in_=kh[pp].rearrange("k p c -> p k c")), deps=dl,
                        sem=dkv[par])
                    for c in range(8):
                        dk_ = [e_pext] if e_pext else []
                        if c == 0:
                            dk_.append(ev_kw)
                            if grp == 0 and p == 0:
                                dk_.append(e_hT_done)
                                dk_.append(e_spill_done)
                            if grp == 1 and p == 0:
                                dk_ += [e_fa_tail, e_exp_last]
                        for k in range(DK):
                            e_kmm = P.t(lambda t_, k=k, c=c, par=par: t_.matmul(
                                p_p, kvw[par][:, 128 * k:128 * (k + 1)],
                                hT_v[:, k, 512 * c:512 * (c + 1)],
                                start=(k == 0), stop=(k == DK - 1)),
                                deps=dk_ if k == 0 else (), inc=(k == DK - 1))
                        e_pext = P.s(lambda s, p=p, c=c: s.activation(
                            kTg[:, 4096 * p + 512 * c:4096 * p + 512 * (c + 1)],
                            p_p, AF.Copy, scale=0.25), deps=[e_kmm], inc=1)
                    ev_vw = P.dma("gpsimd", lambda g, pp=pp, par=par: g.dma_start(
                        out=kvw[par].rearrange("p (k c) -> p k c", k=DK),
                        in_=vh[pp].rearrange("k p c -> p k c")), deps=[e_kmm],
                        sem=dkv[par])
                    e_vext = None
                    for qr in range(4):
                        for cc in range(2):
                            c = 2 * qr + cc
                            dv = [e_vext] if e_vext else []
                            if qr == 0 and cc == 0:
                                dv += [ev_vw, e_pext]
                            for k in range(DK):
                                e_vmm = P.t(lambda t_, k=k, c=c, par=par: t_.matmul(
                                    p_p, kvw[par][:, 128 * k:128 * (k + 1)],
                                    hT_v[:, k, 512 * c:512 * (c + 1)],
                                    start=(k == 0), stop=(k == DK - 1)),
                                    deps=dv if k == 0 else (), inc=(k == DK - 1))
                            e_vext = P.s(lambda s, cc=cc: s.activation(
                                vTt[:, 512 * cc:512 * (cc + 1)], p_p, AF.Copy,
                                scale=0.25), deps=[e_vmm], inc=1)
                        for w in range(2):
                            for jj in range(4):
                                jt = 4 * w + jj
                                dt_ = []
                                if jj == 0:
                                    dt_ = [e_vext]
                                    if e_vgev is not None:
                                        dt_.append(e_vgev)
                                    if grp == 1 and p == 0 and qr == 0 and w == 0:
                                        dt_.append(e_z_read_done)
                                e_vtr = P.t(lambda t_, jt=jt, jj=jj: t_.transpose(
                                    p_vt[:, 128 * jj:128 * (jj + 1)],
                                    vTt[:, 128 * jt:128 * (jt + 1)], idf[:, :]),
                                    deps=dt_, inc=(jj == 3))
                            j0 = 8 * qr + 4 * w
                            e_vgev = P.s(lambda s, j0=j0, p=p: s.copy(
                                vg_v[:, j0:j0 + 4, 2 * p:2 * p + 2, 0:64],
                                p_vt.rearrange("p (j a c) -> p j a c", j=4, a=2)),
                                deps=[e_vtr], inc=1)
                    e_kv_reader[par] = e_vmm
                    e_pp_reader = e_vext
                e_ones = P.v(lambda v: v.memset(vg_v[:, :, :, 64:65], 1.0),
                             deps=[e_vgev], inc=1)
                qw_evs = []
                for p in range(4):
                    pp = 4 * grp + p
                    dl = [e_qmm_last] if (grp == 1 and p == 0 and e_qmm_last) else []
                    if grp == 0 and p == 0:
                        dl += [e_spill_done, e_hk_last]
                    qw_evs.append(P.dma("gpsimd", lambda g, pp=pp, p=p: g.dma_start(
                        out=Qw.rearrange("p (q k c) -> p q k c", q=4, k=DK)[:, p],
                        in_=qh[pp].rearrange("k p c -> p k c")), deps=dl, sem=dq))
                for hl in range(8):
                    hg = 8 * grp + hl
                    dl = [e_z_last] if (grp == 1 and hl == 0 and e_z_last) else []
                    if grp == 0 and hl == 0:
                        dl += [e_spill_done, e_hk_last]
                    qw_evs.append(P.dma("gpsimd", lambda g, hg=hg, hl=hl: g.dma_start(
                        out=Wrhs[:, 1024 * hl:1024 * (hl + 1)], in_=wo[hg]),
                        deps=dl, sem=dq))
                e_qw = qw_evs[-1]

                e_oext_last = None
                for I in range(NSUP_LIM):
                    nj = 4 * I + 4
                    e_qev = {}
                    for p in range(4):
                        dq_ = [e_qext] if e_qext else []
                        if p == 0:
                            dq_.append(e_qw)
                            if I == 0:
                                dq_ += [e_vgev, e_ones, e_pp_reader]
                            else:
                                dq_ += [e_z_last, e_exp_last]
                        for k in range(DK):
                            e_qmm = P.t(lambda t_, k=k, p=p, I=I: t_.matmul(
                                p_p, Qw[:, 1024 * p + 128 * k:1024 * p + 128 * (k + 1)],
                                hT_v[:, k, 512 * I:512 * (I + 1)],
                                start=(k == 0), stop=(k == DK - 1)),
                                deps=dq_ if k == 0 else (), inc=(k == DK - 1))
                        e_qext = P.s(lambda s, p=p: s.activation(
                            qTs[:, 512 * p:512 * (p + 1)], p_p, AF.Copy, scale=0.25),
                            deps=[e_qmm], inc=1)
                        e_qev[p] = e_qext
                        e_qmm_last = e_qmm
                    for duo in range(2):
                        e_exp = {}
                        e_omd = {}
                        for j in range(nj):
                            e_smm_dp = {}
                            for dp in range(2):
                                p = 2 * duo + dp
                                for hh in range(2):
                                    b0 = 64 * hh
                                    sl = 512 * (2 * dp + hh)
                                    ds_ = []
                                    if hh == 0 and dp == 0:
                                        if j == 0:
                                            ds_ = [e_qev[3]]
                                            if duo == 1:
                                                ds_.append(e_oext_last)
                                            elif I > 0 or grp == 1:
                                                ds_.append(e_z_read_done)
                                        else:
                                            ds_ = [e_exp[(j - 1, 0)],
                                                   e_exp[(j - 1, 1)]]
                                    e_smm = P.t(lambda t_, j=j, p=p, b0=b0, sl=sl: t_.matmul(
                                        p_sq[:, sl:sl + 512],
                                        kTg[b0:b0 + 64,
                                            4096 * p + 128 * j:4096 * p + 128 * (j + 1)],
                                        qTs[b0:b0 + 64, 512 * p:512 * (p + 1)],
                                        start=True, stop=True),
                                        deps=ds_, inc=(hh == 1))
                                e_smm_dp[dp] = e_smm
                            for dp in range(2):
                                de_ = [e_smm_dp[dp]]
                                if e_last_om[dp] is not None:
                                    de_.append(e_last_om[dp])
                                e_x = P.s(lambda s, dp=dp: s.activation(
                                    pTq[:, 1024 * dp:1024 * (dp + 1)],
                                    p_sq[:, 1024 * dp:1024 * (dp + 1)], AF.Exp,
                                    scale=1.0 / KD), deps=de_, inc=1)
                                e_exp[(j, dp)] = e_x
                                e_exp_last = e_x
                            m = j - 4 * I
                            e_prdy = {}
                            for dp in range(2):
                                ev = e_exp[(j, dp)]
                                if m >= 0:
                                    for hh in range(2):
                                        o0 = 1024 * dp + 512 * hh
                                        if m > 0:
                                            P.v(lambda v, o0=o0, m=m: v.memset(
                                                pTq[:, o0:o0 + 128 * m], 0.0),
                                                deps=[ev], inc=0)
                                        last = P.v(lambda v, o0=o0, m=m: v.tensor_tensor(
                                            pTq[:, o0 + 128 * m:o0 + 128 * (m + 1)],
                                            pTq[:, o0 + 128 * m:o0 + 128 * (m + 1)],
                                            maskT[:, :], ALU.mult),
                                            deps=[ev], inc=(hh == 1))
                                    ev = last
                                e_prdy[dp] = ev
                            for dp in range(2):
                                for hh in range(2):
                                    hloc = 4 * duo + 2 * dp + hh
                                    oq = 2 * dp + hh
                                    do_ = [e_prdy[dp]] if hh == 0 else []
                                    if j == 0 and hh == 0 and dp == 0:
                                        if duo == 1:
                                            do_.append(e_oext_last)
                                        elif I > 0 or grp == 1:
                                            do_.append(e_z_read_done)
                                    e_om = P.t(lambda t_, j=j, hloc=hloc, oq=oq,
                                               dp=dp, hh=hh, nj=nj: t_.matmul(
                                        p_oq[0:65, 512 * oq:512 * (oq + 1)],
                                        vg_v[:, j, hloc, :],
                                        pTq[:, 1024 * dp + 512 * hh:
                                            1024 * dp + 512 * (hh + 1)],
                                        start=(j == 0), stop=(j == nj - 1)),
                                        deps=do_, inc=(hh == 1))
                                e_omd[(j, dp)] = e_om
                                e_last_om[dp] = e_om
                        for dp in range(2):
                            for hh in range(2):
                                hq = 2 * dp + hh
                                hloc = 4 * duo + 2 * dp + hh
                                first = (dp == 0 and hh == 0)
                                e_li = P.v(lambda v, hq=hq: v.reciprocal(
                                    sc1[64:65, :],
                                    p_oq[64:65, 512 * hq:512 * (hq + 1)]),
                                    deps=[e_om], inc=1)
                                lr_deps = [e_li]
                                if first:
                                    lr_deps += [e_exp[(nj - 1, 0)],
                                                e_exp[(nj - 1, 1)]]
                                e_lrep = P.t(lambda t_: t_.matmul(
                                    p_sq[:, 0:512], ones_f[64:65, :], sc1[64:65, :],
                                    start=True, stop=True), deps=lr_deps, inc=1)
                                e_oc = P.s(lambda s, hq=hq: s.copy(
                                    osb_b[:, :], p_oq[0:64, 512 * hq:512 * (hq + 1)]),
                                    deps=[e_om] + ([e_oxp] if e_oxp else []), inc=1)
                                e_oxp = P.v(lambda v, hloc=hloc: v.tensor_tensor(
                                    oTs[:, 512 * hloc:512 * (hloc + 1)], osb_b[:, :],
                                    p_sq[0:64, 0:512], ALU.mult),
                                    deps=[e_lrep, e_oc], inc=1)
                        e_oext_last = e_oxp
                    for tt in range(4):
                        t = 4 * I + tt
                        zpar = tt % 2
                        for hl in range(8):
                            dz_ = []
                            if hl == 0:
                                dz_ = [e_oext_last]
                                if tt == 0:
                                    dz_.append(e_qw)
                                if e_z_read_done is not None:
                                    dz_.append(e_z_read_done)
                            for n in range(2):
                                e_z = P.t(lambda t_, tt=tt, hl=hl, n=n: t_.matmul(
                                    p_z[:, 512 * n:512 * (n + 1)],
                                    oTs[:, 512 * hl + 128 * tt:
                                        512 * hl + 128 * (tt + 1)],
                                    Wrhs[:, 1024 * hl + 512 * n:
                                         1024 * hl + 512 * (n + 1)],
                                    start=(hl == 0), stop=(hl == 7)),
                                    deps=dz_ if n == 0 else (),
                                    inc=(hl == 7 and n == 1))
                        e_zcp = P.v(lambda v, zpar=zpar: v.tensor_scalar_mul(
                            zb[zpar][:, :], p_z, 0.25),
                            deps=[e_z] + ([rmw_by_par[zpar]] if rmw_by_par[zpar] else []),
                            inc=1)
                        e_zc[zpar] = e_zcp
                        ev_rmw = P.dma("gpsimd", lambda g, t=t, zpar=zpar: g.dma_start(
                            out=x_sp.rearrange("(t p) d -> p t d", p=128)[:, t, :],
                            in_=zb[zpar][:, :], accum_op=ALU.add),
                            deps=[e_zcp], sem=drmw[zpar])
                        rmw_evs.append(ev_rmw)
                        rmw_by_par[zpar] = ev_rmw
                        e_z_last = e_zcp
                        e_z_read_done = e_zcp
                e_fa_tail = e_zcp
        drel = sems["d_rel"]
        rel = []
        for i in range(8):
            rel.append(P.dma("sync", lambda s, i=i: s.dma_start(
                out=xs.rearrange("p (t d) -> p t d", t=T)[:, 4 * i:4 * i + 4, :],
                in_=x_sp.rearrange("(t p) d -> p t d", p=128)[:, 4 * i:4 * i + 4, :]),
                deps=rmw_evs if i == 0 else (), sem=drel))
        return rel[-1]

    last_deps = [e_xload]
    if ONLY == "ffn1":
        evs = emit_ffn(1, [e_xload])
        P.flush()
        last_deps = [evs[-1]]
    if ONLY == "attnb":
        e_rel = emit_attn_b(e_xload)
        P.flush()
        last_deps = [e_rel]
    if ONLY is None and N_SUB >= 1:
        evs = emit_attn_a()
        P.flush()
        last_deps = [evs[-1]]
    if ONLY is None and N_SUB >= 2:
        evs = emit_ffn(1, [])
        P.flush()
        last_deps = [evs[-1]]
    if ONLY is None and N_SUB >= 3:
        e_rel = emit_attn_b(last_deps[-1])
        P.flush()
        last_deps = [e_rel]
    if ONLY is None and N_SUB >= 4:
        evs = emit_ffn(2, [last_deps[-1]])
        P.flush()
        last_deps = [evs[-1]]

    dout = sems["d_out"]
    st = []
    for i in range(8):
        st.append(P.dma("sync", lambda s, i=i: s.dma_start(
            out=out.rearrange("(t p) d -> p t d", p=128)[:, 4 * i:4 * i + 4, :],
            in_=xs.rearrange("p (t d) -> p t d", t=T)[:, 4 * i:4 * i + 4, :]),
            deps=last_deps if i == 0 else (), sem=dout))
    for nm in DBG:
        ap = TENS[nm]
        sh = list(ap.tensor.shape)
        dt = ap.tensor.dtype
        dbg_out = nc.dram_tensor("dbg_" + nm, sh, dt, kind="ExternalOutput").ap()
        st.append(P.dma("sync", lambda s, dbg_out=dbg_out, ap=ap: s.dma_start(
            out=dbg_out[:, :], in_=ap[:, :]), deps=last_deps, sem=dout))
    P.ev("sync", lambda s: s.nop(), deps=st, inc=0)
    P.flush()
    return nc


def kernel(L0_attn_a_x, L0_attn_a_w_q, L0_attn_a_w_k, L0_attn_a_w_v1,
           L0_attn_a_w_v2, L0_attn_b_Q, L0_attn_b_K, L0_attn_b_V, L0_attn_b_W,
           L0_ffn_Wg, L0_ffn_W1, L0_ffn_W2, L0_ffn_W3):
    x = np.ascontiguousarray(np.asarray(L0_attn_a_x, dtype=np.float32))

    def pair_pack(M):
        Mr = np.transpose(np.asarray(M, np.float32), (2, 1, 0))
        Mp = Mr.reshape(8, 2, D, KD)
        Mp = np.transpose(Mp, (0, 2, 1, 3)).reshape(8, D, 128)
        return np.ascontiguousarray(Mp.reshape(8, DK, 128, 128))

    qh = pair_pack(L0_attn_b_Q)
    kh = pair_pack(L0_attn_b_K)
    vh = pair_pack(L0_attn_b_V)
    wo = np.ascontiguousarray(np.transpose(np.asarray(L0_attn_b_W, np.float32),
                                           (2, 1, 0)))
    wgt = np.ascontiguousarray(np.asarray(L0_ffn_Wg, np.float32).T)

    def w12_tile(W):
        Wr = np.asarray(W, np.float32).reshape(E, HCN, 128, DK, 128)
        return np.ascontiguousarray(np.transpose(Wr, (0, 1, 3, 4, 2)))

    w1h = w12_tile(L0_ffn_W1)
    w2h = w12_tile(L0_ffn_W2)
    w3h = np.ascontiguousarray(
        np.transpose(np.asarray(L0_ffn_W3, np.float32), (0, 2, 1))
        .reshape(E, HCN, 128, D))

    ident = np.eye(128, dtype=np.float32)
    maskm = np.triu(np.ones((128, 128), dtype=np.float32))
    onesv = np.ones((128, 128), dtype=np.float32)

    nc = build_program()
    common = dict(
        wq=np.ascontiguousarray(np.asarray(L0_attn_a_w_q, np.float32)),
        wk=np.ascontiguousarray(np.asarray(L0_attn_a_w_k, np.float32)),
        wv1=np.ascontiguousarray(np.asarray(L0_attn_a_w_v1, np.float32)),
        wv2=np.ascontiguousarray(np.asarray(L0_attn_a_w_v2, np.float32)),
        qh=qh, kh=kh, vh=vh, wo=wo, wgt=wgt,
        w1h=w1h, w2h=w2h, w3h=w3h, ident=ident, mask=maskm, onesv=onesv,
    )
    in_maps = [dict(common, x_in=np.ascontiguousarray(x[b])) for b in range(8)]
    res = run_bass_kernel_spmd(nc, in_maps, core_ids=list(range(8)))
    global LAST_RESULTS
    LAST_RESULTS = res.results
    return np.stack([res.results[b]["out"] for b in range(8)])
